# version 05
# baseline (speedup 1.0000x reference)
import math
import os
import numpy as np

import concourse.bass as bass
import concourse.mybir as mybir
from concourse.bass_utils import run_bass_kernel_spmd

F32 = mybir.dt.float32
BF16 = mybir.dt.bfloat16
AF = mybir.ActivationFunctionType
ALU = mybir.AluOpType
AX = mybir.AxisListType

BATCH = 2
L = 2048
D = 2048
NST = 8
DTR = 256
INTER = 8192
R = 512
TILES = (BATCH * L) // R
TPB = L // R
NDB = D // 128
NMB = INTER // 128
LN_EPS = 1e-5
C_FWHT1 = 1.0 / (math.sqrt(2048.0) * 64.0)
C_FWHT2 = 1.0 / math.sqrt(8192.0)
RECIP_CAP = 1e37

SB_BASE = 17408
O_XNT = SB_BASE
O_UT = SB_BASE + 32768
O_VT = SB_BASE + 49152
O_SHR = SB_BASE + 114688
O_DA = [O_SHR + 0, O_SHR + 2048]
O_H = [O_SHR + 4096, O_SHR + 6144]
O_DBU = O_SHR + 8192
O_G = O_SHR + 10240
O_Z = O_SHR + 12288
O_T1 = O_SHR + 14336
O_DELTA = [O_SHR + 16384, O_SHR + 18432]
O_YB = [O_SHR + 20480, O_SHR + 22528, O_SHR + 24576]
O_O2 = [O_SHR + 26624, O_SHR + 28672]
O_BREP = O_SHR + 30720
O_CREP = O_SHR + 38912
O_SDBC = O_SHR + 47104
O_MEANBC = O_SHR + 49152
O_WP = O_SHR + 51200
O_XLOAD = [O_SHR + 0, O_SHR + 8192]
O_XNSQ = O_SHR + 16384
O_BFLY = O_SHR + 0
O_WRING = O_SHR + 53248
O_WXPT = SB_BASE + 184320
O_WDTT = SB_BASE + 193024
O_DTRT = SB_BASE + 201216
O_XNB = SB_BASE + 203264
O_OSTG = SB_BASE + 205312
O_CARRY = SB_BASE + 206336
O_CONST = SB_BASE + 206848
O_A = O_CONST
O_BDT = O_CONST + 512
O_SUD = O_CONST + 576
O_SU = O_CONST + 640
O_SUDP = O_CONST + 704
O_H128 = O_CONST + 960
O_IDF32 = O_CONST + 1216
O_ONES = O_CONST + 1728
O_ONESR = O_CONST + 1760
O_SDROW = O_XNB
O_MNROW = O_DTRT
O_STATX = O_CONST + 2272
O_QBUF = O_CONST + 2400
O_END = O_CONST + 4448
SBUF_LIMIT = 229376
assert O_END <= SBUF_LIMIT, O_END
O_STAT = None


def popc(v):
    return bin(v).count("1")


class Sem:
    def __init__(self, es, nc, name):
        self.h = es.enter_context(nc.semaphore(name))
        self.v = 0


class Bank:
    def __init__(self, es, nc, name):
        self.t = nc.alloc_psum_tensor(name, [128, 512], F32)
        self.ready = Sem(es, nc, name + "r")
        self.free = Sem(es, nc, name + "f")


class Prog:
    def __init__(self):
        self.q = {k: [] for k in ("sync", "tensor", "vector", "scalar", "gpsimd")}

    def add(self, eng, fn):
        self.q[eng].append(fn)


def build_program(n_tiles=TILES):
    phase = int(os.environ.get("K_PHASE", "99"))
    scanlvl = int(os.environ.get("K_SCANLVL", "4"))
    nc = bass.Bass("TRN2", target_bir_lowering=False, debug=False,
                   detect_race_conditions=False)

    x_in = nc.dram_tensor("x", [BATCH, L, D], F32, kind="ExternalInput").ap()
    wxp = nc.dram_tensor("wxp", [DTR + 2 * NST, D], F32, kind="ExternalInput").ap()
    wdt = nc.dram_tensor("wdt", [D, DTR], F32, kind="ExternalInput").ap()
    bdt_in = nc.dram_tensor("bdt", [D], F32, kind="ExternalInput").ap()
    alog = nc.dram_tensor("alog", [D, NST], F32, kind="ExternalInput").ap()
    dpar = nc.dram_tensor("dpar", [D], F32, kind="ExternalInput").ap()
    suug = nc.dram_tensor("suug", [D], F32, kind="ExternalInput").ap()
    sudp_in = nc.dram_tensor("sudp", [INTER], F32, kind="ExternalInput").ap()
    wsug = nc.dram_tensor("wsug", [2 * INTER], F32, kind="ExternalInput").ap()
    wsdp = nc.dram_tensor("wsdp", [D], F32, kind="ExternalInput").ap()
    wup = nc.dram_tensor("wup", [INTER, D], F32, kind="ExternalInput").ap()
    wgate = nc.dram_tensor("wgate", [INTER, D], F32, kind="ExternalInput").ap()
    wdown = nc.dram_tensor("wdown", [D, INTER], F32, kind="ExternalInput").ap()
    out = nc.dram_tensor("out", [BATCH, L, D], F32, kind="ExternalOutput").ap()

    wupT_sc = nc.dram_tensor("wupT_sc", [NMB, 16, 128, 128], BF16, kind="Internal").ap()
    wgT_sc = nc.dram_tensor("wgT_sc", [NMB, 16, 128, 128], BF16, kind="Internal").ap()
    wdT_sc = nc.dram_tensor("wdT_sc", [NDB, NMB, 128, 128], BF16, kind="Internal").ap()
    dbg = os.environ.get("K_DBGOUT", "") == "1"
    bc_sc = nc.dram_tensor("bc_sc", [16, R], BF16,
                           kind="ExternalOutput" if dbg else "Internal").ap()
    y_sc = nc.dram_tensor("y_sc", [NDB, 128, R], F32,
                          kind="ExternalOutput" if dbg else "Internal").ap()
    ydbg = nc.dram_tensor("ydbg", [NDB, 4, 128, R], F32,
                          kind="ExternalOutput" if dbg else "Internal").ap()

    import ml_dtypes
    ii = np.arange(128)
    h128_np = np.where((np.bitwise_count(ii[:, None] & ii[None, :]) % 2) == 0,
                       np.float32(1.0), np.float32(-1.0))
    h128_c = nc.inline_tensor(h128_np.astype(ml_dtypes.bfloat16), name="h128c")
    idf32_c = nc.inline_tensor(np.eye(128, dtype=np.float32), name="idf32c")
    onesr_c = nc.inline_tensor(np.ones((1, 128), dtype=np.float32), name="onesrc")
    sgn64_np = np.array([[-1.0 if popc(j) % 2 else 1.0 for j in range(64)]] * 128,
                        dtype=np.float32)
    sgn64_c = nc.inline_tensor(sgn64_np, name="sgn64c")

    import contextlib
    es = contextlib.ExitStack()
    with es:
        def sb(name, shape, dtype, off):
            return nc.alloc_sbuf_tensor_at(name, shape, dtype, offset=off)

        xnT = sb("xnT", [128, NDB * R], F32, O_XNT)
        uT = sb("uT", [128, NDB * R], BF16, O_UT)
        vT = sb("vT", [128, NMB * R], BF16, O_VT)
        xload = [sb(f"xload{i}", [128, D], F32, O_XLOAD[i]) for i in range(2)]
        xnsq = sb("xnsq", [128, D], F32, O_XNSQ)
        bfly = sb("bfly", [128, NDB * R], F32, O_BFLY)
        dAb = [sb(f"dA{i}", [128, R], F32, O_DA[i]) for i in range(2)]
        hb = [sb(f"hb{i}", [128, R], F32, O_H[i]) for i in range(2)]
        dbu = sb("dbu", [128, R], F32, O_DBU)
        gbuf = sb("gbuf", [128, R], F32, O_G)
        zbuf = sb("zbuf", [128, R], F32, O_Z)
        t1b = sb("t1b", [128, R], F32, O_T1)
        dlt = [sb(f"dlt{i}", [128, R], F32, O_DELTA[i]) for i in range(2)]
        yb = [sb(f"yb{i}", [128, R], F32, O_YB[i]) for i in range(3)]
        o2 = [sb(f"o2_{i}", [128, R], F32, O_O2[i]) for i in range(2)]
        brep = sb("brep", [128, NST * R], BF16, O_BREP)
        crep = sb("crep", [128, NST * R], BF16, O_CREP)
        sdbc = sb("sdbc", [128, R], F32, O_SDBC)
        meanbc = sb("meanbc", [128, R], F32, O_MEANBC)
        wpb = sb("wpb", [128, R], F32, O_WP)
        wring = [sb(f"wring{i}", [128, 2048], BF16, O_WRING + 4096 * i) for i in range(4)]
        wxpT = sb("wxpT", [128, NDB * 272], BF16, O_WXPT)
        wdtT = sb("wdtT", [128, 2 * 2048], BF16, O_WDTT)
        dtrT = sb("dtrT", [128, 2 * R], BF16, O_DTRT)
        xnb = [sb(f"xnb{i}", [128, R], BF16, O_XNB + 1024 * i) for i in range(2)]
        ostg = [sb(f"ostg{i}", [128, 128], F32, O_OSTG + 512 * i) for i in range(2)]
        carry2 = sb("carry2", [128, NDB * NST], F32, O_CARRY)
        A_t = sb("A_t", [128, 128], F32, O_A)
        bdt_t = sb("bdt_t", [128, 16], F32, O_BDT)
        sud_t = sb("sud_t", [128, 16], F32, O_SUD)
        su_t = sb("su_t", [128, 16], F32, O_SU)
        sudp_t = sb("sudp_t", [128, 64], F32, O_SUDP)
        h128_t = sb("h128_t", [128, 128], BF16, O_H128)
        idf32_t = sb("idf32_t", [128, 128], F32, O_IDF32)
        stat_t = sb("stat_t", [128, 32], F32, O_STATX)
        ones_t = sb("ones_t", [128, 1], F32, O_ONES)
        sdrow = sb("sdrow", [1, 512], F32, O_SDROW)
        mnrow = sb("mnrow", [1, 512], F32, O_MNROW)
        ebuf = sb("ebuf", [128, R], F32, O_XNB)
        qbuf = sb("qbuf", [128, R], F32, O_QBUF)
        rbuf = sb("rbuf", [128, R], F32, O_O2[0])
        upsb = sb("upsb", [128, R], F32, O_O2[1])
        bc16 = sb("bc16", [16, R], BF16, O_O2[1])
        pA = [sb(f"pA{i}", [128, D], F32, O_SHR + 8192 * i) for i in range(2)]
        pS = sb("pS", [128, D], F32, O_SHR + 16384)
        pstg = [sb(f"pstg{i}", [128, D], BF16, O_SHR + 24576 + 4096 * i) for i in range(2)]
        vstage = sb("vstage", [128, 128], F32, O_SHR + 36864)
        vec2 = sb("vec2", [128, 128], F32, O_SHR + 37376)
        sgn64_t = sb("sgn64_t", [128, 64], F32, O_SHR + 37888)
        wsdp_t = sb("wsdp_t", [128, 16], F32, O_SHR + 38144)
        sudps_t = sb("sudps_t", [128, 64], F32, O_SHR + 38400)
        onesr_t = sb("onesr_t", [1, 128], F32, O_ONESR)
        pdp = sb("pdp", [128, INTER], F32, O_XNT)
        pdt = [sb(f"pdt{i}", [128, 128], BF16, O_UT + 256 * i) for i in range(2)]
        pds = sb("pds", [128, NMB * 128], BF16, O_VT)

        banks = [Bank(es, nc, f"bk{i}") for i in range(8)]
        upb = banks[0:2]
        gtb = banks[2:4]
        shb = banks[4:8]

        def S(name):
            return Sem(es, nc, name)

        s_xin = S("sxin"); s_sq = S("ssq"); s_var = S("svar"); s_rstd = S("srstd")
        s_xn = S("sxn"); s_xfree = S("sxfree"); s_xnT = S("sxnT"); s_u = S("su")
        s_uT = S("suT"); s_xnb = S("sxnb"); s_xdblk = S("sxdblk"); s_dtr = S("sdtr")
        s_bcev = S("sbcev"); s_bco = S("sbco"); s_bci = S("sbci")
        s_sp = S("ssp"); s_spf = S("sspf"); s_delta = S("sdelta")
        s_dadone = S("sdadone"); s_dA = S("sdA"); s_h = S("sh"); s_g = S("sg")
        s_yi = S("syi"); s_y = S("sy"); s_ywr = S("sywr"); s_yrd = S("syrd")
        s_e = S("se"); s_ups = S("sups"); s_v = S("sv")
        s_den = S("sden"); s_r = S("sr")
        s_w = S("sw"); s_wfree = S("swfree")
        s_o2 = S("so2"); s_otr = S("sotr"); s_ost = S("sost"); s_odma = S("sodma")
        s_prep = S("sprep"); s_pdve = S("spdve"); s_ppe = S("sppe")
        s_ones = S("sones")
        s_pev = S("spev"); s_pout = S("spout"); s_row = S("srow")

        P = Prog()
        SY, TE, VE, SC, GP = "sync", "tensor", "vector", "scalar", "gpsimd"

        def dma(dst, src, sem, waits=()):
            def f(e, dst=dst, src=src, sem=sem, waits=tuple(waits)):
                for (sh, sv) in waits:
                    if sv > 0:
                        e.wait_ge(sh, sv)
                e.dma_start(out=dst, in_=src).then_inc(sem.h, 16)
            P.add(SY, f)
            sem.v += 16

        def f(e):
            e.memset(ones_t[:, :], 1.0).then_inc(s_ones.h, 1)
        P.add(VE, f)
        s_ones.v += 1
        dma(h128_t[:, :], h128_c.ap()[:, :], s_prep)
        dma(idf32_t[:, :], idf32_c.ap()[:, :], s_prep)
        dma(onesr_t[:, :], onesr_c.ap()[:, :], s_prep)
        dma(sgn64_t[:, :], sgn64_c.ap()[:, :], s_prep)
        for db in range(NDB):
            dma(A_t[:, db * 8:(db + 1) * 8], alog[db * 128:(db + 1) * 128, :], s_prep)
        dma(vstage[0:16, :], bdt_in.rearrange("(a p) -> a p", p=128), s_prep)
        dma(vstage[16:32, :], dpar.rearrange("(a p) -> a p", p=128), s_prep)
        dma(vstage[32:48, :], suug.rearrange("(a p) -> a p", p=128), s_prep)
        dma(vstage[48:64, :], wsdp.rearrange("(a p) -> a p", p=128), s_prep)
        dma(vstage[64:128, :], sudp_in.rearrange("(a p) -> a p", p=128), s_prep)
        wv0 = s_prep.v

        b0 = shb[2]

        def f(e, wv=wv0, b=b0, fv=b0.free.v):
            e.wait_ge(s_prep.h, wv)
            e.transpose(b.t[:, 0:128], vstage[:, :], idf32_t[:, :]).then_inc(b.ready.h, 1)
        P.add(TE, f)
        b0.ready.v += 1

        def f(e, rv=b0.ready.v, b=b0):
            e.wait_ge(b.ready.h, rv)
            e.tensor_copy(out=bdt_t[:, :], in_=b.t[:, 0:16])
            e.tensor_copy(out=su_t[:, :], in_=b.t[:, 32:48])
            e.tensor_copy(out=sudp_t[:, :], in_=b.t[:, 64:128])
            e.tensor_scalar_mul(out=wsdp_t[:, :], in0=b.t[:, 48:64], scalar1=C_FWHT2)
            i = e.tensor_mul(out=sudps_t[:, :], in0=b.t[:, 64:128], in1=sgn64_t[:, :])
            i.then_inc(s_pdve.h, 1)
        P.add(VE, f)
        s_pdve.v += 1

        def f(e, pv=s_pdve.v, b=b0):
            e.wait_ge(s_pdve.h, pv)
            i = e.tensor_mul(out=sud_t[:, :], in0=su_t[:, :], in1=b.t[:, 16:32])
            i.then_inc(b.free.h, 1)
        P.add(VE, f)
        b0.free.v += 1

        def f(e, wv=wv0):
            e.wait_ge(s_prep.h, wv)
            e.activation(out=A_t[:, :], in_=A_t[:, :], func=AF.Exp).then_inc(s_ppe.h, 1)
        P.add(SC, f)
        s_ppe.v += 1

        def f(e, pv=s_ppe.v):
            e.wait_ge(s_ppe.h, pv)
            e.tensor_scalar_mul(out=A_t[:, :], in0=A_t[:, :], scalar1=-1.0)
        P.add(VE, f)

        dma(vstage[:, :], wsug.rearrange("(a p) -> a p", p=128), s_prep,
            waits=[(b0.free.h, b0.free.v)])
        wv1 = s_prep.v
        b1 = shb[3]

        def f(e, wv=wv1, b=b1, fv=b1.free.v):
            e.wait_ge(s_prep.h, wv)
            e.wait_ge(b.free.h, fv)
            e.transpose(b.t[:, 0:128], vstage[:, :], idf32_t[:, :]).then_inc(b.ready.h, 1)
        P.add(TE, f)
        b1.ready.v += 1

        def f(e, rv=b1.ready.v, b=b1):
            e.wait_ge(b.ready.h, rv)
            i = e.tensor_scalar_mul(out=vec2[:, :], in0=b.t[:, 0:128], scalar1=64.0)
            i.then_inc(b.free.h, 1)
        P.add(VE, f)
        b1.free.v += 1
        vec2_ready = (b1.free.h, b1.free.v)

        prot = [0]

        def prep_rotbank():
            b = shb[0 + (prot[0] % 2)]
            prot[0] += 1
            return b

        panel_cnt = [0]
        pa_free = [None, None]
        ps_free = [None]

        for (r0, nr, cbase) in ((0, 128, 0), (128, 128, 128), (256, 16, 256)):
            slot = panel_cnt[0] % 2
            panel_cnt[0] += 1
            dst = pA[slot]
            dma(dst[0:nr, :], wxp[r0:r0 + nr, :], s_prep,
                waits=[pa_free[slot]] if pa_free[slot] else [])
            wv = s_prep.v
            for grp in range(4):
                b = prep_rotbank()
                fv = b.free.v
                for q in range(4):
                    db = grp * 4 + q

                    def f(e, b=b, db=db, q=q, dst=dst, nr=nr, wv=wv, fv=fv, first=(q == 0)):
                        if first:
                            e.wait_ge(b.free.h, fv)
                            e.wait_ge(s_prep.h, wv)
                        i = e.transpose(b.t[:, q * 128:q * 128 + nr],
                                        dst[0:nr, db * 128:(db + 1) * 128],
                                        idf32_t[0:nr, 0:nr])
                        if q == 3:
                            i.then_inc(b.ready.h, 1)
                    P.add(TE, f)
                b.ready.v += 1
                rv = b.ready.v

                def f(e, b=b, grp=grp, rv=rv, cbase=cbase, nr=nr):
                    e.wait_ge(b.ready.h, rv)
                    for q in range(4):
                        db = grp * 4 + q
                        i = e.tensor_scalar_mul(
                            out=wxpT[:, db * 272 + cbase: db * 272 + cbase + nr],
                            in0=b.t[:, q * 128:q * 128 + nr],
                            scalar1=su_t[:, db:db + 1])
                    i.then_inc(b.free.h, 1)
                P.add(VE, f)
                b.free.v += 1
                if grp == 3:
                    pa_free[slot] = (b.ready.h, b.ready.v)

        for db in range(NDB):
            slot = panel_cnt[0] % 2
            panel_cnt[0] += 1
            dst = pA[slot]
            dma(dst[:, 0:256], wdt[db * 128:(db + 1) * 128, :], s_prep,
                waits=[pa_free[slot]] if pa_free[slot] else [])
            wv = s_prep.v
            b = prep_rotbank()
            fv = b.free.v

            def f(e, b=b, dst=dst, wv=wv, fv=fv):
                e.wait_ge(b.free.h, fv)
                e.wait_ge(s_prep.h, wv)
                e.transpose(b.t[:, 0:128], dst[:, 0:128], idf32_t[:, :])
                i = e.transpose(b.t[:, 128:256], dst[:, 128:256], idf32_t[:, :])
                i.then_inc(b.ready.h, 1)
            P.add(TE, f)
            b.ready.v += 1
            rv = b.ready.v

            def f(e, b=b, db=db, rv=rv):
                e.wait_ge(b.ready.h, rv)
                e.tensor_copy(out=wdtT[:, db * 128: (db + 1) * 128], in_=b.t[:, 0:128])
                i = e.tensor_copy(out=wdtT[:, 2048 + db * 128: 2048 + (db + 1) * 128],
                                  in_=b.t[:, 128:256])
                i.then_inc(b.free.h, 1)
            P.add(VE, f)
            b.free.v += 1
            pa_free[slot] = (b.ready.h, b.ready.v)

        for w_ap, sc_base, sc_dst in ((wup, 0, wupT_sc), (wgate, 64, wgT_sc)):
            for mb in range(NMB):
                slot = panel_cnt[0] % 2
                panel_cnt[0] += 1
                dst = pA[slot]
                dma(dst[:, :], w_ap[mb * 128:(mb + 1) * 128, :], s_prep,
                    waits=[pa_free[slot]] if pa_free[slot] else [])
                wv = s_prep.v

                def f(e, dst=dst, wv=wv, si=sc_base + mb, pf=ps_free[0]):
                    e.wait_ge(s_prep.h, wv)
                    if pf is not None:
                        e.wait_ge(pf[0], pf[1])
                    e.tensor_scalar_mul(out=pS[:, :], in0=dst[:, :],
                                        scalar1=vec2[:, si:si + 1]).then_inc(s_pdve.h, 1)
                P.add(VE, f)
                s_pdve.v += 1
                pdv = s_pdve.v
                stg = pstg[mb % 2]
                stg_free_at = max(0, s_pout.v - 16)
                for grp in range(4):
                    b = prep_rotbank()
                    fv = b.free.v
                    for q in range(4):
                        db = grp * 4 + q

                        def f(e, b=b, db=db, q=q, fv=fv, pdv=pdv, first=(q == 0)):
                            if first:
                                e.wait_ge(b.free.h, fv)
                                e.wait_ge(s_pdve.h, pdv)
                            i = e.transpose(b.t[:, q * 128:(q + 1) * 128],
                                            pS[:, db * 128:(db + 1) * 128], idf32_t[:, :])
                            if q == 3:
                                i.then_inc(b.ready.h, 1)
                        P.add(TE, f)
                    b.ready.v += 1
                    rv = b.ready.v

                    def f(e, b=b, grp=grp, rv=rv, stg=stg, sfa=stg_free_at, last=(grp == 3)):
                        e.wait_ge(b.ready.h, rv)
                        if grp == 0 and sfa > 0:
                            e.wait_ge(s_pout.h, sfa)
                        i = e.tensor_copy(out=stg[:, grp * 512:(grp + 1) * 512],
                                          in_=b.t[:, 0:512])
                        i.then_inc(b.free.h, 1)
                    P.add(VE, f)
                    b.free.v += 1
                    if grp == 3:
                        pev_t = (b.free.h, b.free.v)
                        pa_free[slot] = (b.ready.h, b.ready.v)
                        ps_free[0] = (b.ready.h, b.ready.v)

                def f(e, stg=stg, dst2=sc_dst[mb], pev=pev_t):
                    e.wait_ge(pev[0], pev[1])
                    e.dma_start(out=dst2.rearrange("k p f -> p k f"),
                                in_=stg[:, :].rearrange("p (k f) -> p k f", k=16))\
                        .then_inc(s_pout.h, 16)
                P.add(SY, f)
                s_pout.v += 16

        pdp_free = [None]
        for cp in range(NDB):
            for q4 in range(4):
                dma(pdp[:, q4 * 2048:(q4 + 1) * 2048],
                    wdown[cp * 128:(cp + 1) * 128, q4 * 2048:(q4 + 1) * 2048],
                    s_prep,
                    waits=[pdp_free[0]] if (q4 == 0 and pdp_free[0]) else ())
            wv = s_prep.v

            def f(e, cp=cp, wv=wv):
                e.wait_ge(s_prep.h, wv)
                e.tensor_scalar_mul(out=pdp[:, :], in0=pdp[:, :],
                                    scalar1=wsdp_t[:, cp:cp + 1])
                for st in range(6):
                    dist = 32 >> st
                    ng = 64 // (2 * dist)
                    view = pdp[:, :].rearrange("p (g two i) -> p g two i", g=ng, two=2)
                    Av = view[:, :, 0, :]
                    Bv = view[:, :, 1, :]
                    e.tensor_add(out=Av, in0=Av, in1=Bv)
                    i = e.scalar_tensor_tensor(out=Bv, in0=Bv, scalar=2.0, in1=Av,
                                               op0=ALU.mult, op1=ALU.subtract)
                i.then_inc(s_pdve.h, 1)
            P.add(VE, f)
            s_pdve.v += 1
            pdv = s_pdve.v
            for mb in range(NMB):
                b = prep_rotbank()
                fv = b.free.v

                def f(e, b=b, mb=mb, fv=fv, pdv=pdv):
                    e.wait_ge(b.free.h, fv)
                    if mb == 0:
                        e.wait_ge(s_pdve.h, pdv)
                    e.transpose(b.t[:, 0:128], pdp[:, mb * 128:(mb + 1) * 128],
                                idf32_t[:, :]).then_inc(b.ready.h, 1)
                P.add(TE, f)
                b.ready.v += 1
                if mb == NMB - 1:
                    pdp_free[0] = (b.ready.h, b.ready.v)
                rv = b.ready.v

                def f(e, b=b, mb=mb, rv=rv):
                    e.wait_ge(b.ready.h, rv)
                    i = e.tensor_copy(out=pdt[mb % 2][:, :], in_=b.t[:, 0:128])
                    i.then_inc(b.free.h, 1)
                P.add(VE, f)
                b.free.v += 1
                pdv2 = (b.free.h, b.free.v)
                bm = shb[2 + (mb % 2)]
                fv2 = bm.free.v

                def f(e, bm=bm, mb=mb, fv2=fv2, pdv2=pdv2):
                    e.wait_ge(bm.free.h, fv2)
                    e.wait_ge(pdv2[0], pdv2[1])
                    e.matmul(bm.t[:, 0:128], h128_t[:, :], pdt[mb % 2][:, :],
                             start=True, stop=True).then_inc(bm.ready.h, 1)
                P.add(TE, f)
                bm.ready.v += 1
                rv2 = bm.ready.v

                def f(e, bm=bm, mb=mb, rv2=rv2, last=(mb == NMB - 1),
                      pw=s_pout.v if mb == 0 else 0):
                    e.wait_ge(bm.ready.h, rv2)
                    if pw > 0:
                        e.wait_ge(s_pout.h, pw)
                    i = e.tensor_scalar_mul(out=pds[:, mb * 128:(mb + 1) * 128],
                                            in0=bm.t[:, 0:128],
                                            scalar1=sudps_t[:, mb:mb + 1])
                    i.then_inc(bm.free.h, 1)
                P.add(VE, f)
                bm.free.v += 1
                if mb == NMB - 1:
                    pev_t2 = (bm.free.h, bm.free.v)

            def f(e, cp=cp, pev=pev_t2):
                e.wait_ge(pev[0], pev[1])
                e.dma_start(out=wdT_sc[cp].rearrange("k p f -> p k f"),
                            in_=pds[:, :].rearrange("p (k f) -> p k f", k=NMB))\
                    .then_inc(s_pout.h, 16)
            P.add(SY, f)
            s_pout.v += 16

        if os.environ.get("K_CDBG", "") == "1":
            def f(e):
                e.tensor_copy(out=zbuf[:, 0:16], in_=sud_t[:, :])
                e.tensor_copy(out=zbuf[:, 16:32], in_=bdt_t[:, :])
                e.tensor_copy(out=zbuf[:, 32:48], in_=su_t[:, :])
                e.tensor_copy(out=zbuf[:, 48:112], in_=sudp_t[:, :])
                e.tensor_copy(out=zbuf[:, 112:240], in_=A_t[:, :])
                i = e.tensor_copy(out=zbuf[:, 240:368], in_=vec2[:, :])
                i.then_inc(s_pdve.h, 1)
            P.add(VE, f)
            s_pdve.v += 1

            def f(e, pv=s_pdve.v):
                e.wait_ge(s_pdve.h, pv)
                e.dma_start(out=ydbg[15, 3], in_=zbuf[:, :]).then_inc(s_prep.h, 16)
            P.add(SY, f)
            s_prep.v += 16

        prep_done_pout = s_pout.v

        wslot_free_at = [None, None, None, None]
        o2_free_tuple = [None, None]
        scanj = [0]
        dajg = [0]
        gc_cnt = [0]

        for t in range(n_tiles):
            if phase < 1:
                break
            bt = t // TPB
            lc = t % TPB
            row0 = lc * R
            first_of_batch = (lc == 0)

            bar = dict(g=128 * t, dad=16 * t, ywr=16 * 16 * t, otr=16 * t)

            xn_tr_done = [None]
            for rc in range(4):
                slot = rc % 2
                gidx = t * 4 + rc
                waits = []
                if gidx >= 2:
                    waits.append((s_xn.h, gidx - 2 + 1))
                if t == 0:
                    if rc == 0:
                        waits.append((s_pout.h, prep_done_pout))
                else:
                    waits.append((s_g.h, bar["g"]))
                    waits.append((s_dA.h, 128 * t))
                    waits.append((s_ywr.h, bar["ywr"]))
                    waits.append((s_ost.h, 48 * t))
                dma(xload[slot][:, :],
                    x_in[bt, row0 + rc * 128: row0 + (rc + 1) * 128, :],
                    s_xin, waits=waits)
                xv = s_xin.v

                def f(e, slot=slot, xv=xv):
                    e.wait_ge(s_xin.h, xv)
                    e.tensor_reduce(out=stat_t[:, 16:17], in_=xload[slot][:, :],
                                    axis=AX.X, op=ALU.add).then_inc(s_var.h, 1)
                P.add(VE, f)
                s_var.v += 1

                def f(e, slot=slot, xv=xv, tr=xn_tr_done[0]):
                    e.wait_ge(s_xin.h, xv)
                    if tr is not None:
                        e.wait_ge(tr[0], tr[1])
                    e.activation(out=xnsq[:, :], in_=xload[slot][:, :],
                                 func=AF.Square).then_inc(s_sq.h, 1)
                P.add(SC, f)
                s_sq.v += 1
                sqv = s_sq.v

                def f(e, sqv=sqv, rc=rc):
                    e.wait_ge(s_sq.h, sqv)
                    e.tensor_reduce(out=stat_t[:, 17:18], in_=xnsq[:, :],
                                    axis=AX.X, op=ALU.add)
                    e.tensor_scalar_mul(out=stat_t[:, 18:19], in0=stat_t[:, 16:17],
                                        scalar1=1.0 / D)
                    e.tensor_scalar(out=stat_t[:, 19:20], in0=stat_t[:, 17:18],
                                    scalar1=1.0 / D, scalar2=LN_EPS,
                                    op0=ALU.mult, op1=ALU.add)
                    e.tensor_mul(out=stat_t[:, 20:21], in0=stat_t[:, 18:19],
                                 in1=stat_t[:, 18:19])
                    e.tensor_copy(out=stat_t[:, 28 + rc:29 + rc],
                                  in_=stat_t[:, 18:19])
                    i = e.tensor_sub(out=stat_t[:, 19:20], in0=stat_t[:, 19:20],
                                     in1=stat_t[:, 20:21])
                    i.then_inc(s_var.h, 1)
                P.add(VE, f)
                s_var.v += 1
                vv = s_var.v

                def f(e, vv=vv, rc=rc):
                    e.wait_ge(s_var.h, vv)
                    e.activation(out=stat_t[:, 20:21], in_=stat_t[:, 19:20],
                                 func=AF.Ln)
                    e.activation(out=stat_t[:, 21:22], in_=stat_t[:, 20:21],
                                 func=AF.Exp, scale=-0.5)
                    i = e.activation(out=stat_t[:, 24 + rc:25 + rc],
                                     in_=stat_t[:, 20:21],
                                     func=AF.Exp, scale=0.5)
                    i.then_inc(s_rstd.h, 1)
                P.add(SC, f)
                s_rstd.v += 1
                rsv = s_rstd.v

                def f(e, slot=slot, rsv=rsv):
                    e.wait_ge(s_rstd.h, rsv)
                    e.tensor_mul(out=stat_t[:, 23:24], in0=stat_t[:, 18:19],
                                 in1=stat_t[:, 21:22])
                    i = e.tensor_scalar(out=xnsq[:, :], in0=xload[slot][:, :],
                                        scalar1=stat_t[:, 21:22],
                                        scalar2=stat_t[:, 23:24],
                                        op0=ALU.mult, op1=ALU.subtract)
                    i.then_inc(s_xn.h, 1)
                P.add(VE, f)
                s_xn.v += 1
                xnv = s_xn.v

                bS, bM = shb[2], shb[3]
                fS, fM = bS.free.v, bM.free.v

                def f(e, rc=rc, xnv=xnv, bS=bS, bM=bM, fS=fS, fM=fM):
                    e.wait_ge(s_xn.h, xnv)
                    if rc == 0:
                        e.wait_ge(bS.free.h, fS)
                        e.wait_ge(bM.free.h, fM)
                    iS = e.transpose(bS.t[0:1, rc * 128:(rc + 1) * 128],
                                     stat_t[:, 24 + rc:25 + rc], idf32_t[:, :])
                    iM = e.transpose(bM.t[0:1, rc * 128:(rc + 1) * 128],
                                     stat_t[:, 28 + rc:29 + rc], idf32_t[:, :])
                    if rc == 3:
                        iS.then_inc(bS.ready.h, 1)
                        iM.then_inc(bM.ready.h, 1)
                P.add(TE, f)
                for grp in range(4):
                    b = shb[grp % 2]
                    fv = b.free.v
                    for q in range(4):
                        db = grp * 4 + q

                        def f(e, b=b, db=db, q=q, fv=fv, xnv=xnv, first=(q == 0)):
                            if first:
                                e.wait_ge(b.free.h, fv)
                                e.wait_ge(s_xn.h, xnv)
                            i = e.transpose(b.t[:, q * 128:(q + 1) * 128],
                                            xnsq[:, db * 128:(db + 1) * 128],
                                            idf32_t[:, :])
                            if q == 3:
                                i.then_inc(b.ready.h, 1)
                        P.add(TE, f)
                    b.ready.v += 1
                    rv = b.ready.v

                    def f(e, b=b, grp=grp, rc=rc, rv=rv, last=(grp == 3 and rc == 3)):
                        e.wait_ge(b.ready.h, rv)
                        for q in range(4):
                            db = grp * 4 + q
                            i = e.tensor_scalar_mul(
                                out=xnT[:, db * R + rc * 128: db * R + (rc + 1) * 128],
                                in0=b.t[:, q * 128:(q + 1) * 128],
                                scalar1=su_t[:, db:db + 1])
                        i.then_inc(b.free.h, 1)
                    P.add(VE, f)
                    b.free.v += 1
                    if grp == 3:
                        xn_tr_done[0] = (b.ready.h, b.ready.v)
            shb[2].ready.v += 1
            shb[3].ready.v += 1

            def f(e, rS=shb[2].ready.v, rM=shb[3].ready.v):
                e.wait_ge(shb[2].ready.h, rS)
                e.wait_ge(shb[3].ready.h, rM)
                i = e.tensor_copy(out=sdrow[0:1, :], in_=shb[2].t[0:1, 0:512])
                i.then_inc(shb[2].free.h, 1)
                i = e.tensor_copy(out=mnrow[0:1, :], in_=shb[3].t[0:1, 0:512])
                i.then_inc(shb[3].free.h, 1)
            P.add(VE, f)
            shb[2].free.v += 1
            shb[3].free.v += 1

            def f(e, f2=shb[2].free.v, f3=shb[3].free.v):
                e.wait_ge(shb[2].free.h, f2)
                e.wait_ge(shb[3].free.h, f3)
                e.matmul(shb[2].t[:, 0:512], onesr_t[0:1, :], sdrow[0:1, :],
                         start=True, stop=True).then_inc(shb[2].ready.h, 1)
                e.matmul(shb[3].t[:, 0:512], onesr_t[0:1, :], mnrow[0:1, :],
                         start=True, stop=True).then_inc(shb[3].ready.h, 1)
            P.add(TE, f)
            shb[2].ready.v += 1
            shb[3].ready.v += 1

            def f(e, r2=shb[2].ready.v, r3=shb[3].ready.v):
                e.wait_ge(shb[2].ready.h, r2)
                e.wait_ge(shb[3].ready.h, r3)
                i = e.tensor_copy(out=sdbc[:, :], in_=shb[2].t[:, 0:512])
                i.then_inc(shb[2].free.h, 1)
                i = e.tensor_copy(out=meanbc[:, :], in_=shb[3].t[:, 0:512])
                i.then_inc(shb[3].free.h, 1)
            P.add(VE, f)
            shb[2].free.v += 1
            shb[3].free.v += 1

            if phase < 2:
                continue
            def f(e, gbar=bar["g"], t=t):
                if t > 0:
                    e.wait_ge(s_g.h, gbar)
                    e.wait_ge(s_ost.h, 48 * t)
                    e.wait_ge(s_ywr.h, 256 * t)
                Av = xnT[:, 0:8 * R]
                Bv = xnT[:, 8 * R:16 * R]
                e.tensor_add(out=bfly[:, 0:8 * R], in0=Av, in1=Bv)
                e.tensor_sub(out=bfly[:, 8 * R:16 * R], in0=Bv, in1=Av)
                for dist in (4, 2):
                    ng = 16 // (2 * dist)
                    view = bfly[:, :].rearrange("p (g two i) -> p g two i", g=ng, two=2)
                    Av = view[:, :, 0, :]
                    Bv = view[:, :, 1, :]
                    e.tensor_add(out=Av, in0=Av, in1=Bv)
                    e.scalar_tensor_tensor(out=Bv, in0=Bv, scalar=2.0, in1=Av,
                                           op0=ALU.mult, op1=ALU.subtract)
                view = bfly[:, :].rearrange("p (g two i) -> p g two i", g=8, two=2)
                uview = uT[:, :].rearrange("p (g two i) -> p g two i", g=8, two=2)
                Av = view[:, :, 0, :]
                Bv = view[:, :, 1, :]
                e.tensor_add(out=uview[:, :, 0, :], in0=Av, in1=Bv)
                i = e.tensor_sub(out=uview[:, :, 1, :], in0=Bv, in1=Av)
                i.then_inc(s_u.h, 1)
            P.add(VE, f)
            s_u.v += 1
            uv = s_u.v

            for db in range(NDB):
                b = shb[db % 2]
                fv = b.free.v

                def f(e, b=b, db=db, fv=fv, uv=uv):
                    e.wait_ge(b.free.h, fv)
                    if db == 0:
                        e.wait_ge(s_u.h, uv)
                    e.matmul(b.t[:, 0:512], h128_t[:, :], uT[:, db * R:(db + 1) * R],
                             start=True, stop=True).then_inc(b.ready.h, 1)
                P.add(TE, f)
                b.ready.v += 1
                rv = b.ready.v
                sgn = -C_FWHT1 if popc(db) % 2 else C_FWHT1

                def f(e, b=b, db=db, rv=rv, sgn=sgn, last=(db == NDB - 1)):
                    e.wait_ge(b.ready.h, rv)
                    i = e.activation(out=uT[:, db * R:(db + 1) * R], in_=b.t[:, 0:512],
                                     func=AF.Copy, scale=sgn)
                    i.then_inc(b.free.h, 1)
                P.add(SC, f)
                b.free.v += 1
            uT_tuple = (shb[(NDB - 1) % 2].free.h, shb[(NDB - 1) % 2].free.v)

            if phase < 3:
                continue
            f0, f1, f2 = shb[0].free.v, shb[1].free.v, shb[2].free.v
            xdblk_base = s_xdblk.v
            for kd in range(NDB):
                xb_need = xdblk_base + kd - 1

                def f(e, kd=kd, xbn=xb_need):
                    if kd >= 2:
                        e.wait_ge(s_xdblk.h, xbn)
                    e.tensor_copy(out=xnb[kd % 2][:, :],
                                  in_=xnT[:, kd * R:(kd + 1) * R]).then_inc(s_xnb.h, 1)
                P.add(VE, f)
                s_xnb.v += 1
                xv = s_xnb.v

                def f(e, kd=kd, xv=xv, f0=f0, f1=f1, f2=f2):
                    e.wait_ge(s_xnb.h, xv)
                    if kd == 0:
                        e.wait_ge(shb[0].free.h, f0)
                        e.wait_ge(shb[1].free.h, f1)
                        e.wait_ge(shb[2].free.h, f2)
                    st = (kd == 0)
                    sp = (kd == NDB - 1)
                    i0 = e.matmul(shb[0].t[:, 0:512], wxpT[:, kd * 272:kd * 272 + 128],
                                  xnb[kd % 2][:, :], start=st, stop=sp)
                    i1 = e.matmul(shb[1].t[:, 0:512],
                                  wxpT[:, kd * 272 + 128:kd * 272 + 256],
                                  xnb[kd % 2][:, :], start=st, stop=sp)
                    i2 = e.matmul(shb[2].t[0:16, 0:512],
                                  wxpT[:, kd * 272 + 256:kd * 272 + 272],
                                  xnb[kd % 2][:, :], start=st, stop=sp)
                    if sp:
                        i0.then_inc(shb[0].ready.h, 1)
                        i1.then_inc(shb[1].ready.h, 1)
                        i2.then_inc(shb[2].ready.h, 1)
                    else:
                        i2.then_inc(s_xdblk.h, 1)
                P.add(TE, f)
                if kd < NDB - 1:
                    s_xdblk.v += 1
            shb[0].ready.v += 1
            shb[1].ready.v += 1
            shb[2].ready.v += 1

            def f(e, r0=shb[0].ready.v, r1=shb[1].ready.v):
                e.wait_ge(shb[0].ready.h, r0)
                i = e.activation(out=dtrT[:, 0:512], in_=shb[0].t[:, 0:512], func=AF.Copy)
                i.then_inc(shb[0].free.h, 1)
                e.wait_ge(shb[1].ready.h, r1)
                i = e.activation(out=dtrT[:, 512:1024], in_=shb[1].t[:, 0:512],
                                 func=AF.Copy)
                i.then_inc(shb[1].free.h, 1)
            P.add(SC, f)
            shb[0].free.v += 1
            shb[1].free.v += 1
            dtr_tuple = (shb[1].free.h, shb[1].free.v)

            def f(e, r2=shb[2].ready.v):
                e.wait_ge(shb[2].ready.h, r2)
                i = e.tensor_copy(out=bc16[0:16, :], in_=shb[2].t[0:16, 0:512])
                i.then_inc(shb[2].free.h, 1)
            P.add(VE, f)
            shb[2].free.v += 1

            dma(bc_sc[:, :], bc16[0:16, :], s_bco,
                waits=[(shb[2].free.h, shb[2].free.v)])
            bcov = s_bco.v
            for n in range(2 * NST):
                tgt = brep if n < NST else crep
                col = (n % NST) * R
                dma(tgt[:, col:col + R], bc_sc[n:n + 1, :].partition_broadcast(128),
                    s_bci, waits=[(s_bco.h, bcov)] if n == 0 else ())
            bciv = s_bci.v

            if phase < 4:
                continue
            def emit_delta_scan(c, t=t, bciv=bciv, dtr_tuple=dtr_tuple,
                                first_of_batch=first_of_batch):
                gc = gc_cnt[0]
                gc_cnt[0] += 1
                bk = shb[c % 2]
                fv = bk.free.v

                def f(e, c=c, bk=bk, fv=fv, dtr_tuple=dtr_tuple):
                    e.wait_ge(bk.free.h, fv)
                    if c == 0:
                        e.wait_ge(dtr_tuple[0], dtr_tuple[1])
                    e.matmul(bk.t[:, 0:512], wdtT[:, c * 128:(c + 1) * 128],
                             dtrT[:, 0:512], start=True, stop=False)
                    i = e.matmul(bk.t[:, 0:512], wdtT[:, 2048 + c * 128: 2048 + (c + 1) * 128],
                                 dtrT[:, 512:1024], start=False, stop=True)
                    i.then_inc(bk.ready.h, 1)
                P.add(TE, f)
                bk.ready.v += 1
                rv = bk.ready.v
                if scanlvl < 1:
                    def f(e, c=c, bk=bk, rv=rv):
                        e.wait_ge(bk.ready.h, rv)
                        e.tensor_copy(out=dlt[c % 2][:, :], in_=bk.t[:, 0:512])\
                            .then_inc(bk.free.h, 1)
                    P.add(VE, f)
                    bk.free.v += 1
                    return
                spf_need = s_delta.v

                spmode = int(os.environ.get("K_SPMODE", "3"))

                def f(e, c=c, bk=bk, rv=rv, spn=spf_need, ov=s_ones.v,
                      first_sp=(gc == 0), spmode=spmode):
                    e.wait_ge(bk.ready.h, rv)
                    if first_sp:
                        e.wait_ge(s_ones.h, ov)
                    if spn > 0:
                        e.wait_ge(s_delta.h, spn)
                    if spmode >= 1:
                        i = e.activation(out=t1b[:, :], in_=bk.t[:, 0:512], func=AF.Abs,
                                         bias=bdt_t[:, c:c + 1])
                    else:
                        i = e.copy(out=t1b[:, :], in_=bk.t[:, 0:512])
                    i.then_inc(bk.free.h, 1)
                    if spmode >= 2:
                        e.activation(out=t1b[:, :], in_=t1b[:, :], func=AF.Exp,
                                     scale=-1.0)
                    if spmode >= 3:
                        i = e.activation(out=t1b[:, :], in_=t1b[:, :], func=AF.Ln,
                                         bias=ones_t[:, 0:1])
                    else:
                        i = e.copy(out=zbuf[:, 0:4], in_=t1b[:, 0:4])
                    i.then_inc(s_sp.h, 1)
                P.add(SC, f)
                bk.free.v += 1
                act_read_done = bk.free.v
                s_sp.v += 1
                spv = s_sp.v
                dsl = dlt[c % 2]
                dad_need = NST * (gc - 1)
                if os.environ.get("K_NOFAKEDA", "") == "1":
                    dad_need = 0
                ywr_need = 16 * (gc - 3 + 1)

                dvmode = int(os.environ.get("K_DVMODE", "3"))

                def f(e, c=c, dsl=dsl, spv=spv, dadn=dad_need, ywn=ywr_need,
                      bk=bk, rv=rv, dvmode=dvmode, ard=act_read_done):
                    e.wait_ge(bk.ready.h, rv)
                    e.wait_ge(bk.free.h, ard)
                    if dadn > 0:
                        e.wait_ge(s_dA.h, dadn)
                    if dvmode >= 1:
                        i = e.tensor_scalar(out=dsl[:, :], in0=bk.t[:, 0:512],
                                            scalar1=bdt_t[:, c:c + 1], scalar2=0.0,
                                            op0=ALU.add, op1=ALU.max)
                    else:
                        i = e.tensor_copy(out=dsl[:, :], in_=bk.t[:, 0:512])
                    i.then_inc(bk.free.h, 1)
                    e.wait_ge(s_sp.h, spv)
                    if dvmode >= 2:
                        i = e.tensor_add(out=dsl[:, :], in0=dsl[:, :], in1=t1b[:, :])
                    else:
                        i = e.tensor_copy(out=zbuf[:, 4:8], in_=t1b[:, 4:8])
                    i.then_inc(s_delta.h, 1)
                    if dvmode >= 3:
                        e.tensor_mul(out=wpb[:, :], in0=dsl[:, :],
                                     in1=xnT[:, c * R:(c + 1) * R])
                    if scanlvl >= 4:
                        if ywn > 0:
                            e.wait_ge(s_ywr.h, ywn)
                        i = e.scalar_tensor_tensor(out=yb[c % 3][:, :],
                                                   in0=xnT[:, c * R:(c + 1) * R],
                                                   scalar=sud_t[:, c:c + 1],
                                                   in1=meanbc[:, :],
                                                   op0=ALU.mult, op1=ALU.add)
                        i.then_inc(s_yi.h, 1)
                P.add(VE, f)
                bk.free.v += 1
                s_delta.v += 1
                if scanlvl >= 4:
                    s_yi.v += 1
                dlv = s_delta.v
                yiv = s_yi.v
                if scanlvl < 2:
                    for n in range(0 if os.environ.get("K_NOFAKEDA", "") == "1" else NST):
                        def f(e, c=c, n=n, dlv=dlv):
                            if n == 0:
                                e.wait_ge(s_delta.h, dlv)
                            e.copy(out=dAb[0][:, 0:4], in_=dlt[c % 2][:, 0:4])\
                                .then_inc(s_dA.h, 1)
                        P.add(SC, f)
                        s_dA.v += 1
                        scanj[0] += 1
                    if os.environ.get("K_NOYDMA", "") != "1":
                        dma(y_sc[c], dlt[c % 3 % 2][:, :], s_ywr,
                            waits=[(s_dA.h, s_dA.v)])
                    return
                for n in range(NST):
                    j = scanj[0]
                    scanj[0] += 1
                    par = j % 2
                    h_need = j - 1

                    def f(e, c=c, n=n, par=par, dlv=dlv, hn=h_need, j=j,
                          last_n=(n == NST - 1)):
                        if j >= 2:
                            e.wait_ge(s_h.h, hn)
                        if n == 0:
                            e.wait_ge(s_delta.h, dlv)
                        i = e.activation(out=dAb[par][:, :], in_=dlt[c % 2][:, :],
                                         func=AF.Exp,
                                         scale=A_t[:, c * 8 + n: c * 8 + n + 1])
                        i.then_inc(s_dA.h, 1)
                    P.add(SC, f)
                    s_dA.v += 1
                    dav = s_dA.v
                    g_need = j - 1

                    def f(e, c=c, n=n, par=par, dav=dav, gn=g_need, j=j,
                          fob=first_of_batch, bciv=bciv):
                        if j % (NST * NDB) == 0:
                            e.wait_ge(s_bci.h, bciv)
                        e.scalar_tensor_tensor(out=dbu[:, :], in0=wpb[:, :],
                                               scalar=su_t[:, c:c + 1],
                                               in1=brep[:, n * R:(n + 1) * R],
                                               op0=ALU.mult, op1=ALU.mult)
                        e.wait_ge(s_dA.h, dav)
                        if j >= 2:
                            e.wait_ge(s_g.h, gn)
                        if scanlvl >= 3:
                            init = 0.0 if fob else carry2[:, c * 8 + n: c * 8 + n + 1]
                            i = e.tensor_tensor_scan(out=hb[par][:, :],
                                                     data0=dAb[par][:, :],
                                                     data1=dbu[:, :], initial=init,
                                                     op0=ALU.mult, op1=ALU.add)
                            i.then_inc(s_h.h, 1)
                            e.wait_ge(s_h.h, j + 1)
                            e.tensor_copy(out=carry2[:, c * 8 + n: c * 8 + n + 1],
                                          in_=hb[par][:, 511:512])
                        else:
                            e.tensor_copy(out=hb[par][:, :], in_=dbu[:, :])\
                                .then_inc(s_h.h, 1)
                    P.add(VE, f)
                    s_h.v += 1
                    hv = s_h.v

                    def f(e, c=c, n=n, par=par, hv=hv, yiv=yiv, lvl=scanlvl):
                        e.wait_ge(s_h.h, hv)
                        if lvl >= 4:
                            if n == 0:
                                e.wait_ge(s_yi.h, yiv)
                            e.tensor_mul(out=gbuf[:, :], in0=hb[par][:, :],
                                         in1=crep[:, n * R:(n + 1) * R])
                            i = e.tensor_add(out=yb[c % 3][:, :],
                                             in0=yb[c % 3][:, :], in1=gbuf[:, :])
                            i.then_inc(s_g.h, 1)
                        else:
                            e.tensor_copy(out=gbuf[:, 0:4], in_=hb[par][:, 0:4])\
                                .then_inc(s_g.h, 1)
                    P.add(VE if os.environ.get("K_POOL2VE", "") == "1" else GP, f)
                    s_g.v += 1
                    if os.environ.get("K_HDBG", "") == "1" and t == 0 and c < 2:
                        dma(ydbg[c * 8 + n - (0 if c == 0 else 0) if False else (c * 8 + n) // 4,
                                 (c * 8 + n) % 4],
                            hb[par][:, :], s_ywr, waits=[(s_g.h, s_g.v)])
                if scanlvl >= 4:
                    yw_waits = [(s_g.h, s_g.v)]
                    if t > 0:
                        yw_waits.append((s_yrd.h, 16 * (16 * (t - 1) + c + 1)))
                    dma(y_sc[c], yb[c % 3][:, :], s_ywr, waits=yw_waits)
                    if os.environ.get("K_YDBG", "") == "1" and t == 0:
                        par_last = (scanj[0] - 1) % 2
                        dma(ydbg[c, 0], dAb[par_last][:, :], s_ywr,
                            waits=[(s_g.h, s_g.v)])
                        dma(ydbg[c, 1], dbu[:, :], s_ywr, waits=[(s_g.h, s_g.v)])
                        dma(ydbg[c, 2], hb[par_last][:, :], s_ywr,
                            waits=[(s_g.h, s_g.v)])
                        dma(ydbg[c, 3], dlt[c % 2][:, :], s_ywr,
                            waits=[(s_g.h, s_g.v)])
                else:
                    dma(y_sc[c], hb[0][:, :], s_ywr, waits=[(s_g.h, s_g.v)])

            noscan = os.environ.get("K_NOSCAN", "") == "1"
            for mb in range(NMB):
                if mb % 4 == 0 and not noscan:
                    emit_delta_scan(mb // 4)
                slot_u = 2 * (mb % 2)
                slot_g = slot_u + 1
                su_, sg_ = wring[slot_u], wring[slot_g]
                dma(su_[:, :].rearrange("p (k f) -> p k f", k=16),
                    wupT_sc[mb].rearrange("k p f -> p k f"), s_w,
                    waits=[wslot_free_at[slot_u]] if wslot_free_at[slot_u] else [])
                dma(sg_[:, :].rearrange("p (k f) -> p k f", k=16),
                    wgT_sc[mb].rearrange("k p f -> p k f"), s_w,
                    waits=[wslot_free_at[slot_g]] if wslot_free_at[slot_g] else [])
                wneed = s_w.v
                ub, gb = upb[mb % 2], gtb[mb % 2]
                ufv, gfv = ub.free.v, gb.free.v

                def f(e, mb=mb, su_=su_, sg_=sg_, ub=ub, gb=gb, wneed=wneed,
                      ufv=ufv, gfv=gfv, uT_tuple=uT_tuple):
                    e.wait_ge(s_w.h, wneed)
                    if mb == 0:
                        e.wait_ge(uT_tuple[0], uT_tuple[1])
                    e.wait_ge(ub.free.h, ufv)
                    e.wait_ge(gb.free.h, gfv)
                    for kd in range(16):
                        i = e.matmul(ub.t[:, 0:512], su_[:, kd * 128:(kd + 1) * 128],
                                     uT[:, kd * R:(kd + 1) * R],
                                     start=(kd == 0), stop=(kd == 15))
                    i.then_inc(ub.ready.h, 1)
                    for kd in range(16):
                        i = e.matmul(gb.t[:, 0:512], sg_[:, kd * 128:(kd + 1) * 128],
                                     uT[:, kd * R:(kd + 1) * R],
                                     start=(kd == 0), stop=(kd == 15))
                    i.then_inc(gb.ready.h, 1)
                P.add(TE, f)
                ub.ready.v += 1
                gb.ready.v += 1
                wslot_free_at[slot_u] = (gb.ready.h, gb.ready.v)
                wslot_free_at[slot_g] = (gb.ready.h, gb.ready.v)

                def f(e, mb=mb, ub=ub, gb=gb, urv=ub.ready.v, grv=gb.ready.v):
                    e.wait_ge(gb.ready.h, grv)
                    i = e.activation(out=ebuf[:, :], in_=gb.t[:, 0:512],
                                     func=AF.Exp, scale=-1.0)
                    i.then_inc(s_e.h, 1)
                    e.wait_ge(ub.ready.h, urv)
                    i = e.activation(out=upsb[:, :], in_=ub.t[:, 0:512],
                                     func=AF.Copy)
                    i.then_inc(ub.free.h, 1)
                P.add(SC, f)
                s_e.v += 1
                ub.free.v += 1

                def f(e, ev=s_e.v):
                    e.wait_ge(s_e.h, ev)
                    i = e.tensor_scalar_add(out=ebuf[:, :], in0=ebuf[:, :], scalar1=1.0)
                    i.then_inc(s_den.h, 1)
                P.add(VE, f)
                s_den.v += 1

                def f(e, dv=s_den.v):
                    e.wait_ge(s_den.h, dv)
                    e.activation(out=ebuf[:, :], in_=ebuf[:, :], func=AF.Ln)
                    i = e.activation(out=rbuf[:, :], in_=ebuf[:, :],
                                     func=AF.Exp, scale=-1.0)
                    i.then_inc(s_r.h, 1)
                P.add(SC, f)
                s_r.v += 1

                def f(e, mb=mb, gb=gb, rv=s_r.v, ub=ub, ufv=ub.free.v,
                      grv=gb.ready.v):
                    e.wait_ge(ub.free.h, ufv)
                    e.wait_ge(gb.ready.h, grv)
                    i = e.tensor_mul(out=qbuf[:, :], in0=gb.t[:, 0:512], in1=upsb[:, :])
                    i.then_inc(gb.free.h, 1)
                    e.wait_ge(s_r.h, rv)
                    i = e.tensor_mul(out=vT[:, mb * R:(mb + 1) * R], in0=qbuf[:, :],
                                     in1=rbuf[:, :])
                    i.then_inc(s_v.h, 1)
                P.add(VE, f)
                gb.free.v += 1
                s_v.v += 1

            if phase < 5:
                continue
            vneed = s_v.v
            for c in range(NDB):
                bk = shb[2 + (c % 2)]
                fv = bk.free.v
                chunk_needs = []
                for ch in range(4):
                    slot = ch % 4
                    dma(wring[slot][:, :].rearrange("p (k f) -> p k f", k=16),
                        wdT_sc[c, ch * 16:(ch + 1) * 16].rearrange("k p f -> p k f"),
                        s_w, waits=[wslot_free_at[slot]] if wslot_free_at[slot] else [])
                    chunk_needs.append(s_w.v)

                def f(e, c=c, bk=bk, fv=fv, vneed=vneed, cn=tuple(chunk_needs)):
                    if c == 0:
                        e.wait_ge(s_v.h, vneed)
                    e.wait_ge(bk.free.h, fv)
                    for ch in range(4):
                        e.wait_ge(s_w.h, cn[ch])
                        for jj in range(16):
                            m = ch * 16 + jj
                            i = e.matmul(bk.t[:, 0:512],
                                         wring[ch % 4][:, jj * 128:(jj + 1) * 128],
                                         vT[:, m * R:(m + 1) * R],
                                         start=(m == 0), stop=(m == NMB - 1))
                        if ch < 3:
                            i.then_inc(s_wfree.h, 1)
                        else:
                            i.then_inc(bk.ready.h, 1)
                P.add(TE, f)
                for ch in range(3):
                    s_wfree.v += 1
                    wslot_free_at[ch] = (s_wfree.h, s_wfree.v)
                bk.ready.v += 1
                wslot_free_at[3] = (bk.ready.h, bk.ready.v)

                w2 = [(s_ywr.h, 16 * (16 * t + c + 1))]
                if o2_free_tuple[c % 2]:
                    w2.append(o2_free_tuple[c % 2])
                dma(o2[c % 2][:, :], y_sc[c], s_yrd, waits=w2)
                yrdv = s_yrd.v

                def f(e, c=c, bk=bk, rv=bk.ready.v, yrdv=yrdv):
                    e.wait_ge(bk.ready.h, rv)
                    e.wait_ge(s_yrd.h, yrdv)
                    e.scalar_tensor_tensor(out=zbuf[:, :],
                                           in0=xnT[:, c * R:(c + 1) * R],
                                           scalar=su_t[:, c:c + 1],
                                           in1=sdbc[:, :], op0=ALU.mult, op1=ALU.mult)
                    i = e.tensor_add(out=o2[c % 2][:, :], in0=o2[c % 2][:, :],
                                     in1=bk.t[:, 0:512])
                    i.then_inc(bk.free.h, 1)
                    i = e.tensor_add(out=o2[c % 2][:, :], in0=o2[c % 2][:, :],
                                     in1=zbuf[:, :])
                    i.then_inc(s_o2.h, 1)
                P.add(VE, f)
                bk.free.v += 1
                s_o2.v += 1
                o2v = s_o2.v

                bo = shb[c % 2]
                fvo = bo.free.v

                def f(e, c=c, bo=bo, fvo=fvo, o2v=o2v):
                    e.wait_ge(s_o2.h, o2v)
                    e.wait_ge(bo.free.h, fvo)
                    for rc in range(4):
                        i = e.transpose(bo.t[:, rc * 128:(rc + 1) * 128],
                                        o2[c % 2][:, rc * 128:(rc + 1) * 128],
                                        idf32_t[:, :])
                    i.then_inc(bo.ready.h, 1)
                P.add(TE, f)
                bo.ready.v += 1
                o2_free_tuple[c % 2] = (bo.ready.h, bo.ready.v)

                for rc in range(4):
                    gidx = (t * NDB + c) * 4 + rc
                    od_need = 16 * (gidx - 2 + 1)

                    def f(e, c=c, rc=rc, bo=bo, rv=bo.ready.v, odn=od_need,
                          last=(rc == 3)):
                        if rc == 0:
                            e.wait_ge(bo.ready.h, rv)
                        if odn > 0:
                            e.wait_ge(s_odma.h, odn)
                        i = e.tensor_copy(out=ostg[rc % 2][:, :],
                                          in_=bo.t[:, rc * 128:(rc + 1) * 128])
                        if last:
                            i.then_inc(bo.free.h, 1)
                        else:
                            i.then_inc(s_ost.h, 1)
                    P.add(VE, f)
                    if rc < 3:
                        s_ost.v += 1
                        wv_ = (s_ost.h, s_ost.v)
                    else:
                        bo.free.v += 1
                        wv_ = (bo.free.h, bo.free.v)
                    dma(out[bt, row0 + rc * 128: row0 + (rc + 1) * 128,
                            c * 128:(c + 1) * 128],
                        ostg[rc % 2][:, :], s_odma, waits=[wv_])

        def f(e, total=s_odma.v):
            e.wait_ge(s_odma.h, total)
        P.add(SY, f)

        blk = es.enter_context(nc.Block())
        for eng_name, attr in (("sync", blk.sync), ("tensor", blk.tensor),
                               ("vector", blk.vector), ("scalar", blk.scalar),
                               ("gpsimd", blk.gpsimd)):
            prog = P.q[eng_name]

            def emit(e, prog=prog):
                for fn in prog:
                    fn(e)
            attr(emit)

    for a in nc.m.functions[0].allocations:
        try:
            ml = a.memorylocations[0]
        except Exception:
            continue
        if getattr(ml, "memory", None) is None:
            pass
        nm = ml.name
        if nm.startswith("const-"):
            assert ml.addr + 128 <= SB_BASE, (nm, ml.addr)
    return nc


def kernel(**inputs):
    x = np.ascontiguousarray(inputs["L0_attn_x"])
    args = dict(
        wxp=np.ascontiguousarray(inputs["L0_attn_W_xproj"]),
        wdt=np.ascontiguousarray(inputs["L0_attn_W_dt"]),
        bdt=np.ascontiguousarray(inputs["L0_attn_b_dt"]),
        alog=np.ascontiguousarray(inputs["L0_attn_A_log"]),
        dpar=np.ascontiguousarray(inputs["L0_attn_D_param"]),
        suug=np.ascontiguousarray(inputs["L0_ffn_SU_ug"]),
        sudp=np.ascontiguousarray(inputs["L0_ffn_SU_dp"]),
        wsug=np.ascontiguousarray(inputs["L0_ffn_Wscale_ug"]),
        wsdp=np.ascontiguousarray(inputs["L0_ffn_Wscale_dp"]),
        wup=np.ascontiguousarray(inputs["L0_ffn_W_up"]),
        wgate=np.ascontiguousarray(inputs["L0_ffn_W_gate"]),
        wdown=np.ascontiguousarray(inputs["L0_ffn_W_down"]),
    )
    nc = build_program(TILES)
    in_maps = []
    for core in range(8):
        m = dict(args)
        m["x"] = np.ascontiguousarray(x[2 * core: 2 * core + 2])
        in_maps.append(m)
    res = run_bass_kernel_spmd(nc, in_maps, core_ids=list(range(8)))
    return np.concatenate([r["out"] for r in res.results], axis=0)


if __name__ == "__main__":
    nc = build_program(TILES)
    n = sum(1 for _ in nc.all_instructions())
    print("instructions:", n)
